# version 27
# speedup vs baseline: 1.0538x; 1.0117x over previous
import math
import numpy as np
from contextlib import ExitStack

import concourse.bass as bass
import concourse.mybir as mybir
from concourse.bass_utils import run_bass_kernel_spmd

F32 = mybir.dt.float32
BF16 = mybir.dt.bfloat16
I32 = mybir.dt.int32
AF = mybir.ActivationFunctionType
ALU = mybir.AluOpType
AX = mybir.AxisListType
AP = bass.AP

D = 1024
ND = D // 128
IFF = 2816
NI = IFF // 128
EPS = 1e-5
INV_D = 1.0 / D
SM_SCALE = 1.0 / 32.0
NEG_BIG = -1e30
TWO_PI = 6.283185307179586
INV_2PI = 1.0 / TWO_PI
CW1 = 6.28125
CW2 = float(np.float32(TWO_PI - CW1))
CW3 = float(np.float64(TWO_PI) - np.float64(CW1) - np.float64(np.float32(TWO_PI - CW1)))
ROPE_LOGC = -2.0 * math.log(10000.0) / float(D)
PI = float(np.pi)
HPI = float(np.pi / 2)


class Sem:
    def __init__(self, h):
        self.h = h
        self.n = 0

    def inc(self, instr, k=1):
        instr.then_inc(self.h, k)
        self.n += k
        return self.n


class Builder:
    def __init__(self, nc, es, T, stage, taps, n_cores=8, mode="full"):
        self.mode = mode
        self.rg = [list(range(n_cores))]
        self.nc = nc
        self.es = es
        self.T = T
        self.NT = T // 128
        self.stage = stage
        self.taps = taps
        self.tap_d = {}
        self.sems = {}
        self.mark = {}

    def S(self, name):
        if name not in self.sems:
            self.sems[name] = Sem(self.es.enter_context(self.nc.semaphore(name)))
        return self.sems[name]

    def rec(self, key, sem):
        self.mark[key] = sem.n

    def fence(self, eng, ins, name):
        s = self.S(name)
        s.inc(ins)
        eng.wait_ge(s.h, s.n)

    def w(self, eng, key, sem):
        eng.wait_ge(sem.h, self.mark[key])

    def TE(self, f):
        self.block.tensor(f)

    def V(self, f):
        self.block.vector(f)

    def SC(self, f):
        self.block.scalar(f)

    def G(self, f):
        self.block.gpsimd(f)

    def SY(self, f):
        self.block.sync(f)

    def sb(self, name, shape, dt, stack=None, side=None):
        return (stack or self.es).enter_context(
            self.nc.sbuf_tensor(name, shape, dt, side=side))

    def ps(self, name, shape, dt, stack=None):
        return (stack or self.es).enter_context(self.nc.psum_tensor(name, shape, dt))

    def tap(self, name, shape, dt):
        if name in self.taps:
            self.tap_d[name] = self.nc.dram_tensor("tap_" + name, shape, dt,
                                                   kind="ExternalOutput")

    def declare_dram(self):
        nc, T = self.nc, self.T
        m = self.mode
        if m in ("full", "L1"):
            self.x_d = nc.dram_tensor("x", [T, D], F32, kind="ExternalInput")
            self.wqT_d = nc.dram_tensor("wqT", [D, D], F32, kind="ExternalInput")
            self.wkT_d = nc.dram_tensor("wkT", [D, D], F32, kind="ExternalInput")
            self.wvT_d = nc.dram_tensor("wvT", [D, D], F32, kind="ExternalInput")
            self.pr_d = nc.dram_tensor("protT", [128, 128], F32, kind="ExternalInput")
            self.mk_d = nc.dram_tensor("mask128", [128, 128], F32, kind="ExternalInput")
        if m in ("full", "L2"):
            self.wgT_d = nc.dram_tensor("wgT", [D, IFF], F32, kind="ExternalInput")
            self.wuT_d = nc.dram_tensor("wuT", [D, IFF], F32, kind="ExternalInput")
            self.wdT_d = nc.dram_tensor("wdT", [IFF, D], F32, kind="ExternalInput")
            self.wgN_d = nc.dram_tensor("wgN", [IFF, D], F32, kind="ExternalInput")
            self.wuN_d = nc.dram_tensor("wuN", [IFF, D], F32, kind="ExternalInput")
        if m in ("full", "L1"):
            self.wdN_d = nc.dram_tensor("wdN", [D, IFF], F32, kind="ExternalInput")
        self.cx_d = nc.dram_tensor("clip_x", [1], F32, kind="ExternalInput")
        self.cd_d = nc.dram_tensor("clip_down", [1], F32, kind="ExternalInput")
        if m in ("full", "L1", "L2"):
            self.id_d = nc.dram_tensor("ident", [128, 128], F32, kind="ExternalInput")
        m = self.mode
        if m in ("full", "L3"):
            self.out_d = nc.dram_tensor("out", [T, D], F32, kind="ExternalOutput")
        if m == "full":
            self.hsp_d = nc.dram_tensor("h_spill", [IFF, T], F32, kind="Internal")
            self.x2sp_d = nc.dram_tensor("x2_spill", [T, D], BF16, kind="Internal")
            self.cc1i_d = nc.dram_tensor("cc1_in", [1], F32, kind="Internal")
            self.cc1o_d = nc.dram_tensor("cc1_out", [1], F32, kind="Internal")
            self.cc2i_d = nc.dram_tensor("cc2_in", [1], F32, kind="Internal")
            self.cc2o_d = nc.dram_tensor("cc2_out", [1], F32, kind="Internal")
        elif m == "L1":
            self.x2sp_d = nc.dram_tensor("x2_out", [T, D], BF16, kind="ExternalOutput")
            self.xnt_d = nc.dram_tensor("xn_out", [128, self.NT * D], BF16,
                                        kind="ExternalOutput")
            self.gm1_d = nc.dram_tensor("gmax1_out", [1, 1], F32, kind="ExternalOutput")
            self.mrow_d = nc.dram_tensor("mrow_out", [1, D], F32, kind="ExternalOutput")
        elif m == "L2":
            self.xnt_d = nc.dram_tensor("xn_in", [128, self.NT * D], BF16,
                                        kind="ExternalInput")
            self.gm8a_d = nc.dram_tensor("gmax8", [1, 8], F32, kind="ExternalInput")
            self.hsp_d = nc.dram_tensor("h_out", [IFF, T], F32, kind="ExternalOutput")
            self.gm2_d = nc.dram_tensor("hmax_out", [1, 1], F32, kind="ExternalOutput")
            self.wdi_d = nc.dram_tensor("wdi_out", [128, NI * 1024], BF16,
                                        kind="ExternalOutput")
            self.mrow_d = nc.dram_tensor("mrow_in", [1, D], F32, kind="ExternalInput")
        elif m == "L3":
            self.x2sp_d = nc.dram_tensor("x2_in", [T, D], BF16, kind="ExternalInput")
            self.hsp_d = nc.dram_tensor("h_in", [IFF, T], F32, kind="ExternalInput")
            self.gm8b_d = nc.dram_tensor("hmax8", [1, 8], F32, kind="ExternalInput")
            self.wdi_d = nc.dram_tensor("wdi_in", [128, NI * 1024], BF16,
                                        kind="ExternalInput")
            self.mrow_d = nc.dram_tensor("mrow_in", [1, D], F32, kind="ExternalInput")
        self.tap("h1", [128, ND * T], BF16)
        self.tap("cos", [128, ND * T], BF16)
        self.tap("sin", [128, ND * T], BF16)
        self.tap("qf", [128, ND * T], BF16)
        self.tap("kf", [128, ND * T], BF16)
        self.tap("v", [128, self.NT * D], BF16)
        self.tap("x2", [T, D], BF16)
        self.tap("xn", [128, self.NT * D], BF16)
        self.tap("xq", [128, ND * T], BF16)
        self.tap("gm", [1, 8], F32)
        self.tap("hq", [128, NI * T], BF16)

    def alloc_persistent(self):
        T, NT = self.T, self.NT
        self.ident = self.sb("ident_sb", [128, 128], F32)
        self.identb = self.sb("identb_sb", [128, 128], BF16)
        self.protT = self.sb("protT_sb", [128, 128], BF16)
        self.cxs = self.sb("cxs", [1, 2], F32)
        self.sgc = self.sb("sgc", [1, 2], F32)
        self.stat = self.sb("stat", [128, 16], F32)
        self.rowmax = self.sb("rowmax", [128, 2], F32)
        self.negmax = self.sb("negmax", [128, 2], F32)
        self.rowsum = self.sb("rowsum", [128, 2], F32)
        self.recsum = self.sb("recsum", [128, 2], F32)
        self.mend = self.sb("mend", [128, 1], F32)
        self.mendj = self.sb("mendj", [128, 2], F32)
        self.invf = self.sb("invf", [1, D], F32)
        self.gmx = self.sb("gmx", [1, 8], F32)
        self.rsx = self.sb("rsx", [1, 8], F32)
        self.rsxb = self.sb("rsxb", [128, 2], F32)
        self.rshb = self.sb("rshb", [128, 1], F32)
        self.epsc = self.sb("epsc", [128, 1], F32)
        self.i2pic = self.sb("i2pic", [128, 1], F32)
        self.xmc = self.sb("xmc", [128, 4], F32)
        self.gm8 = self.sb("gm8", [1, 8], F32)
        self.masksb = self.sb("masksb", [128, 128], BF16)

    def p0(self):
        S, T, NT = self.S, self.T, self.NT

        @self.SY
        def _(sy):
            for j in range(NT):
                if j >= 2:
                    sy.wait_ge(S("s_ap1").h, j - 1)
                d = sy.dma_start(out=self.xr1.ap()[:, (j % 2) * D:(j % 2) * D + D],
                                 in_=AP(self.x_d, j * 128 * D, [[D, 128], [1, D]]))
                S("dx1").inc(d, 16)

        @self.G
        def _(g):
            S("dconst").inc(g.dma_start(out=self.cxs.ap()[:, 0:1], in_=self.cx_d.ap()), 16)
            S("dconst").inc(g.dma_start(out=self.cxs.ap()[:, 1:2], in_=self.cd_d.ap()), 16)
            S("dconst").inc(g.dma_start(out=self.ident.ap(), in_=self.id_d.ap()), 16)
            S("dconst").inc(g.dma_start(out=self.identb.ap(), in_=self.id_d.ap()), 16)
            S("dconst").inc(g.dma_start(out=self.protT.ap(), in_=self.pr_d.ap()), 16)
            S("dconst").inc(g.dma_start(out=self.masksb.ap(), in_=self.mk_d.ap()), 16)
            g.iota(AP(self.idxh, 0, [[D, 1], [2, 512], [1, 2]]), [[1, 512], [0, 2]],
                   base=0, channel_multiplier=0, allow_small_or_imprecise_dtypes=True)
            g.iota(self.posb.ap(), [[1, T]], base=0,
                   channel_multiplier=0, allow_small_or_imprecise_dtypes=True)
            g.iota(self.mend.ap(), [[0, 1]], base=1, channel_multiplier=1,
                   allow_small_or_imprecise_dtypes=True)
            g.memset(self.onec.ap(), 1.0)
            g.memset(self.gmx.ap(), 0.0)
            g.memset(self.epsc.ap(), EPS)
            ins = g.memset(self.i2pic.ap(), INV_2PI)
            S("g_iota").inc(ins)
            for wd, ws in [(self.wqT_d, self.wqsb), (self.wkT_d, self.wksb)]:
                d = g.dma_start(out=AP(ws, 0, [[ND * D, 128], [D, ND], [1, D]]),
                                in_=AP(wd, 0, [[D, 128], [128 * D, ND], [1, D]]))
                S("dwq").inc(d, 16)
        self.rec("wqk_loaded", S("dwq"))
        self.rec("const_done", S("dconst"))

        @self.SC
        def _(sc):
            sc.wait_ge(S("dconst").h, 32)
            ins = sc.activation(self.sgc.ap()[:, 0:2], self.cxs.ap()[:, 0:2], AF.Sigmoid)
            S("s_sig").inc(ins)
            sc.wait_ge(S("g_iota").h, 1)
            ins = sc.activation(self.invf.ap(), self.idxh.ap(), AF.Exp, scale=ROPE_LOGC)
            S("s_invf").inc(ins)

        @self.TE
        def _(t):
            t.wait_ge(S("s_invf").h, 1)
            t.wait_ge(S("g_iota").h, 1)
            for e in range(ND):
                ins = t.matmul(self.REGT.ap()[:, e:e + 1],
                               self.invf.ap()[0:1, e * 128:e * 128 + 128],
                               self.onec.ap()[0:1, 0:1], start=True, stop=True)
            S("t_invT").inc(ins)

        @self.V
        def _(v):
            v.wait_ge(S("t_invT").h, 1)
            ins = v.tensor_copy(self.invfc.ap(), self.REGT.ap()[:, 0:ND])
            S("v_invc").inc(ins)

    def ln_tile(self, eng_x_slice, sem_ready, nready, j, dst_slice, sbase,
                v_var, s_ap, extra_wait=None, apply_extra_wait=None):
        S = self.S

        @self.V
        def _(v, j=j, sbase=sbase):
            v.wait_ge(sem_ready.h, nready)
            if extra_wait is not None:
                extra_wait(v)
            if self.mark.get((s_ap, j - 2), None) is not None:
                v.wait_ge(S(s_ap).h, self.mark[(s_ap, j - 2)])
            st = self.stat.ap()
            sl = eng_x_slice(v)
            ins = v.reduce_sum(st[:, sbase + 2:sbase + 3], sl, AX.X)
            self.fence(v, ins, "vf")
            v.tensor_scalar(st[:, sbase + 0:sbase + 1], st[:, sbase + 2:sbase + 3],
                            INV_D, None, ALU.mult)
            ins = v.tensor_scalar(st[:, sbase + 1:sbase + 2], st[:, sbase + 2:sbase + 3],
                                  -INV_D, None, ALU.mult)
            S(v_var).inc(ins)

        @self.SC
        def _(sc, j=j, sbase=sbase):
            sc.wait_ge(sem_ready.h, nready)
            if apply_extra_wait is not None:
                apply_extra_wait(sc)
            st = self.stat.ap()
            sl = eng_x_slice(sc)
            ins = sc.activation(dst_slice(sc), sl, AF.Square,
                                accum_out=st[:, sbase + 3:sbase + 4])
            S("s_sq").inc(ins)

        @self.V
        def _(v, j=j, sbase=sbase):
            st = self.stat.ap()
            v.wait_ge(S("s_sq").h, self.S("s_sq").n)
            v.wait_ge(S(v_var).h, S(v_var).n)
            v.tensor_scalar(st[:, sbase + 2:sbase + 3], st[:, sbase + 3:sbase + 4],
                            INV_D, None, ALU.mult)
            ins = v.tensor_tensor(st[:, sbase + 3:sbase + 4], st[:, sbase + 0:sbase + 1],
                                  st[:, sbase + 0:sbase + 1], ALU.mult)
            self.fence(v, ins, "vf")
            ins = v.scalar_tensor_tensor(st[:, sbase + 4:sbase + 5],
                                         st[:, sbase + 2:sbase + 3], EPS,
                                         st[:, sbase + 3:sbase + 4],
                                         ALU.add, ALU.subtract)
            self.fence(v, ins, "vf")
            ins = v.reciprocal(st[:, sbase + 5:sbase + 6], st[:, sbase + 4:sbase + 5])
            S(v_var).inc(ins)

        @self.SC
        def _(sc, j=j, sbase=sbase):
            st = self.stat.ap()
            sc.wait_ge(S(v_var).h, S(v_var).n)
            if apply_extra_wait is not None:
                apply_extra_wait(sc)
            ins = sc.activation(st[:, sbase + 5:sbase + 6], st[:, sbase + 5:sbase + 6],
                                AF.Sqrt)
            self.fence(sc, ins, "sf")
            ins = sc.activation(st[:, sbase + 6:sbase + 7], st[:, sbase + 1:sbase + 2],
                                AF.Identity, scale=st[:, sbase + 5:sbase + 6])
            self.fence(sc, ins, "sf")
            sl = eng_x_slice(sc)
            ins = sc.activation(dst_slice(sc), sl, AF.Identity,
                                scale=st[:, sbase + 5:sbase + 6],
                                bias=st[:, sbase + 6:sbase + 7])
            S(s_ap).inc(ins)
            self.rec((s_ap, j), S(s_ap))

    def p12(self):
        S, T, NT = self.S, self.T, self.NT
        for j in range(NT):
            sbase = (j % 2) * 8
            xs = (lambda e, j=j: self.xr1.ap()[:, (j % 2) * D:(j % 2) * D + D])
            dst = (lambda e, j=j: self.h1r.ap()[:, 0:D])
            aw = None
            if j >= 1:
                aw = (lambda e, j=j: e.wait_ge(S("t_h1T").h, 2 * (j - 1) + 2))
            self.ln_tile(xs, S("dx1"), 16 * (j + 1), j, dst, sbase,
                         "v_var1", "s_ap1", apply_extra_wait=aw)

            @self.TE
            def _(t, j=j):
                t.wait_ge(S("s_ap1").h, self.mark[("s_ap1", j)])
                t.wait_ge(S("dconst").h, 48)
                for grp in range(2):
                    gi = 2 * j + grp
                    base = (gi % 2) * 512
                    if gi >= 2:
                        t.wait_ge(S("v_h1e").h, gi - 1)
                    for q in range(4):
                        dc = grp * 4 + q
                        ins = t.transpose(
                            self.REGQ.ap()[:, base + q * 128:base + q * 128 + 128],
                            self.h1r.ap()[:, dc * 128:dc * 128 + 128],
                            self.ident.ap())
                    S("t_h1T").inc(ins)

            @self.V
            def _(v, j=j):
                for grp in range(2):
                    gi = 2 * j + grp
                    base = (gi % 2) * 512
                    v.wait_ge(S("t_h1T").h, gi + 1)
                    ins = v.tensor_copy(
                        AP(self.hxq, (4 * grp) * T + j * 128,
                           [[ND * T, 128], [T, 4], [1, 128]]),
                        self.REGQ.ap()[:, base:base + 512])
                    S("v_h1e").inc(ins)
        self.rec("h1_done", S("v_h1e"))

    def emit_tables(self, ec):
        S, T = self.S, self.T
        NHB = T // 512
        for hb in range(NHB):
            gidx = ec * NHB + hb

            amax = (self.T - 1) * math.exp(ROPE_LOGC * ((ec * 128) // 2))

            @self.V
            def _(v, ec=ec, hb=hb, gidx=gidx, amax=amax):
                v.wait_ge(S("g_iota").h, 1)
                v.wait_ge(S("v_invc").h, 1)
                if gidx >= 2:
                    v.wait_ge(S("s_sin").h, self.mark[("sin", gidx - 2)])
                kb = (gidx % 2) * 1024
                k0 = self.rbuf.ap()[:, kb:kb + 512]
                k1 = self.rbuf.ap()[:, kb + 512:kb + 1024]
                ins = v.tensor_scalar(k0, self.posb.ap()[:, hb * 512:hb * 512 + 512],
                                      self.invfc.ap()[:, ec:ec + 1], None, ALU.mult)
                self.fence(v, ins, "vf")
                if amax > 3 * PI - 0.2:
                    v.tensor_tensor(self.kif.ap()[:, 0:512], k0,
                                    AP(self.i2pic, 0, [[1, 128], [0, 512]]), ALU.mult)
                    v.tensor_copy(self.kff.ap()[:, 0:512], self.kif.ap()[:, 0:512])
                    v.scalar_tensor_tensor(k0, self.kff.ap()[:, 0:512], -CW1, k0,
                                           ALU.mult, ALU.add)
                    v.scalar_tensor_tensor(k0, self.kff.ap()[:, 0:512], -CW2, k0,
                                           ALU.mult, ALU.add)
                    v.tensor_scalar(self.kff.ap()[:, 0:512], k0, PI, None, ALU.is_gt)
                    v.scalar_tensor_tensor(k0, self.kff.ap()[:, 0:512], -TWO_PI, k0,
                                           ALU.mult, ALU.add)
                    wrap_sin = False
                    wrap_cos = True
                else:
                    wrap_sin = amax > PI - 0.1
                    wrap_cos = amax + HPI > PI - 0.1
                if wrap_sin:
                    v.tensor_scalar(self.kff.ap()[:, 0:512], k0, PI, None, ALU.is_gt)
                    v.scalar_tensor_tensor(k0, self.kff.ap()[:, 0:512], -TWO_PI, k0,
                                           ALU.mult, ALU.add)
                ins = v.tensor_scalar(k1, k0, HPI, None, ALU.add)
                if wrap_cos:
                    v.tensor_scalar(self.kff.ap()[:, 0:512], k1, PI, None, ALU.is_gt)
                    ins = v.scalar_tensor_tensor(k1, self.kff.ap()[:, 0:512], -TWO_PI,
                                                 k1, ALU.mult, ALU.add)
                S("v_tabf").inc(ins)
                self.rec(("tabr", gidx), S("v_tabf"))

            @self.SC
            def _(sc, ec=ec, hb=hb, gidx=gidx):
                sc.wait_ge(S("v_tabf").h, self.mark[("tabr", gidx)])
                if ec >= 2:
                    sc.wait_ge(S("v_uk").h, self.mark[("uk_last", ec - 2)])
                    sc.wait_ge(S("g_rope").h, self.mark[("gk_last", ec - 2)])
                kb = (gidx % 2) * 1024
                ins = sc.activation(
                    AP(self.trig, (ec % 2) * T + hb * 512,
                       [[4 * T, 128], [2 * T, 2], [1, 512]]),
                    self.rbuf.ap()[:, kb:kb + 1024], AF.Sin)
                S("s_sin").inc(ins)
                self.rec(("sin", gidx), S("s_sin"))
        self.rec(("tables", ec), S("s_sin"))

    def emit_wdn(self, hc):
        S = self.S
        QWD = IFF // 4

        def reduce_piece(v, kp):
            hcp, qqp = kp // 4, kp % 4
            sl = (kp % 3) * QWD
            if qqp == 0:
                ins = v.reduce_max(self.swd.ap()[:, hcp:hcp + 1],
                                   self.wdn.ap()[:, sl:sl + QWD],
                                   AX.X, apply_absolute_value=True)
                self.fence(v, ins, "vf")
            else:
                ins = v.reduce_max(self.swd.ap()[:, 8:9],
                                   self.wdn.ap()[:, sl:sl + QWD],
                                   AX.X, apply_absolute_value=True)
                self.fence(v, ins, "vf")
                ins = v.tensor_tensor(self.swd.ap()[:, hcp:hcp + 1],
                                      self.swd.ap()[:, hcp:hcp + 1],
                                      self.swd.ap()[:, 8:9], ALU.max)
                self.fence(v, ins, "vf")
            S("v_wdq").inc(v.engine_nop())
            if qqp == 3:
                S("v_wdm").inc(v.engine_nop())
                self.rec(("wdm", hcp), S("v_wdm"))

        for qq in range(4):
            k = hc * 4 + qq

            @self.SC
            def _(sc, hc=hc, qq=qq, k=k):
                if k >= 3:
                    sc.wait_ge(S("v_wdq").h, k - 2)
                d = sc.dma_start(
                    out=self.wdn.ap()[:, (k % 3) * QWD:(k % 3) * QWD + QWD],
                    in_=AP(self.wdN_d, hc * 128 * IFF + qq * QWD,
                           [[IFF, 128], [1, QWD]]))
                S("dwn").inc(d, 16)
                self.rec(("wdn", k), S("dwn"))

            @self.V
            def _(v, k=k):
                if k >= 1:
                    v.wait_ge(S("dwn").h, self.mark[("wdn", k - 1)])
                    reduce_piece(v, k - 1)

        if hc == ND - 1:
            @self.V
            def _(v):
                v.wait_ge(S("dwn").h, self.mark[("wdn", 4 * ND - 1)])
                reduce_piece(v, 4 * ND - 1)

    def p3(self):
        S, T = self.S, self.T
        NHF = max(T // 1024, 1)
        HW = min(T, 1024)
        self.emit_tables(0)
        for ec in range(ND):
            self.emit_wdn(ec)
            for hf in range(NHF):
                @self.TE
                def _(t, ec=ec, hf=hf):
                    t.wait_ge(S("v_h1e").h, self.mark["h1_done"])
                    t.wait_ge(S("dwq").h, self.mark["wqk_loaded"])
                    if (ec, hf) != (0, 0):
                        t.wait_ge(S("s_eq").h, self.mark.get(("eq_prev", ec, hf), 0))
                    for dc in range(ND):
                        for nb in range(HW // 512):
                            ins = t.matmul(
                                self.REGQ.ap()[:, nb * 512:nb * 512 + 512],
                                self.wqsb.ap()[:, dc * D + ec * 128:dc * D + ec * 128 + 128],
                                AP(self.hxq, dc * T + hf * HW + nb * 512,
                                   [[ND * T, 128], [1, 512]]),
                                start=(dc == 0), stop=(dc == ND - 1))
                    S("t_q").inc(ins)
                    self.rec(("tq", ec, hf), S("t_q"))
                    if (ec, hf) != (0, 0):
                        t.wait_ge(S("s_ek").h, self.mark.get(("ek_prev", ec, hf), 0))
                    for dc in range(ND):
                        for nb in range(HW // 512):
                            ins = t.matmul(
                                self.REGK.ap()[:, nb * 512:nb * 512 + 512],
                                self.wksb.ap()[:, dc * D + ec * 128:dc * D + ec * 128 + 128],
                                AP(self.hxq, dc * T + hf * HW + nb * 512,
                                   [[ND * T, 128], [1, 512]]),
                                start=(dc == 0), stop=(dc == ND - 1))
                    S("t_k").inc(ins)
                    self.rec(("tk", ec, hf), S("t_k"))

                @self.SC
                def _(sc, ec=ec, hf=hf):
                    it = ec * NHF + hf
                    pec, phf = (ec, hf - 1) if hf >= 1 else (ec - 1, NHF - 1)
                    sc.wait_ge(S("t_q").h, self.mark[("tq", ec, hf)])
                    if it >= 1:
                        sc.wait_ge(S("t_rq").h, self.mark[("trq", pec, phf)])
                        sc.wait_ge(S("g_rope").h, self.mark[("gq", pec, phf)])
                    ins = sc.activation(self.qraw.ap()[:, 0:HW],
                                        self.REGQ.ap()[:, 0:HW], AF.Copy)
                    S("s_eq").inc(ins)
                    self.rec(("eq", ec, hf), S("s_eq"))
                    sc.wait_ge(S("t_k").h, self.mark[("tk", ec, hf)])
                    if it >= 1:
                        sc.wait_ge(S("t_rk").h, self.mark[("trk", pec, phf)])
                        sc.wait_ge(S("g_rope").h, self.mark[("gk", pec, phf)])
                    ins = sc.activation(self.kraw.ap()[:, 0:HW],
                                        self.REGK.ap()[:, 0:HW], AF.Copy)
                    S("s_ek").inc(ins)
                    self.rec(("ek", ec, hf), S("s_ek"))
                nec, nhf = (ec, hf + 1) if hf + 1 < NHF else (ec + 1, 0)
                self.mark[("eq_prev", nec, nhf)] = S("s_eq").n
                self.mark[("ek_prev", nec, nhf)] = S("s_ek").n

                @self.TE
                def _(t, ec=ec, hf=hf):
                    t.wait_ge(S("s_eq").h, self.mark[("eq", ec, hf)])
                    if self.mark.get(("ruk_prev", ec, hf)) is not None:
                        t.wait_ge(S("v_uk").h, self.mark[("ruk_prev", ec, hf)])
                    for nb in range(HW // 512):
                        ins = t.matmul(
                            self.REGR.ap()[:, nb * 512:nb * 512 + 512], self.protT.ap(),
                            self.qraw.ap()[:, nb * 512:nb * 512 + 512],
                            start=True, stop=True)
                    S("t_rq").inc(ins)
                    self.rec(("trq", ec, hf), S("t_rq"))

                @self.V
                def _(v, ec=ec, hf=hf):
                    v.wait_ge(S("t_rq").h, self.mark[("trq", ec, hf)])
                    v.wait_ge(S("s_sin").h, self.mark[("tables", ec)])
                    dst = AP(self.qf, ec * T + hf * HW, [[ND * T, 128], [1, HW]])
                    ins = v.tensor_tensor(
                        dst, self.REGR.ap()[:, 0:HW],
                        self.trig.ap()[:, (ec % 2) * T + hf * HW:(ec % 2) * T + hf * HW + HW],
                        ALU.mult)
                    S("v_uq").inc(ins)
                    self.rec(("uq", ec, hf), S("v_uq"))

                @self.G
                def _(g, ec=ec, hf=hf):
                    g.wait_ge(S("s_eq").h, self.mark[("eq", ec, hf)])
                    g.wait_ge(S("s_sin").h, self.mark[("tables", ec)])
                    if self.mark.get(("gq_prev", ec, hf)) is not None:
                        g.wait_ge(S("v_qf").h, self.mark[("gq_prev", ec, hf)])
                    ins = g.tensor_tensor(
                        self.rtmp.ap()[:, 0:HW],
                        self.qraw.ap()[:, 0:HW],
                        self.trig.ap()[:, 2 * T + (ec % 2) * T + hf * HW:2 * T + (ec % 2) * T + hf * HW + HW],
                        ALU.mult)
                    S("g_rope").inc(ins)
                    self.rec(("gq", ec, hf), S("g_rope"))

                @self.V
                def _(v, ec=ec, hf=hf):
                    v.wait_ge(S("g_rope").h, self.mark[("gq", ec, hf)])
                    v.wait_ge(S("v_uq").h, self.mark[("uq", ec, hf)])
                    dst = AP(self.qf, ec * T + hf * HW, [[ND * T, 128], [1, HW]])
                    ins = v.tensor_tensor(dst, dst, self.rtmp.ap()[:, 0:HW],
                                          ALU.add)
                    S("v_qf").inc(ins)
                    self.rec(("vqf", ec, hf), S("v_qf"))

                @self.TE
                def _(t, ec=ec, hf=hf):
                    t.wait_ge(S("s_ek").h, self.mark[("ek", ec, hf)])
                    t.wait_ge(S("v_uq").h, self.mark[("uq", ec, hf)])
                    for nb in range(HW // 512):
                        ins = t.matmul(
                            self.REGR.ap()[:, nb * 512:nb * 512 + 512], self.protT.ap(),
                            self.kraw.ap()[:, nb * 512:nb * 512 + 512],
                            start=True, stop=True)
                    S("t_rk").inc(ins)
                    self.rec(("trk", ec, hf), S("t_rk"))

                @self.V
                def _(v, ec=ec, hf=hf):
                    v.wait_ge(S("t_rk").h, self.mark[("trk", ec, hf)])
                    dst = AP(self.kf, ec * T + hf * HW, [[ND * T, 128], [1, HW]])
                    ins = v.tensor_tensor(
                        dst, self.REGR.ap()[:, 0:HW],
                        self.trig.ap()[:, (ec % 2) * T + hf * HW:(ec % 2) * T + hf * HW + HW],
                        ALU.mult)
                    S("v_uk").inc(ins)
                    self.rec(("uk", ec, hf), S("v_uk"))
                nec2, nhf2 = (ec, hf + 1) if hf + 1 < NHF else (ec + 1, 0)
                self.mark[("ruk_prev", nec2, nhf2)] = S("v_uk").n

                @self.G
                def _(g, ec=ec, hf=hf):
                    g.wait_ge(S("s_ek").h, self.mark[("ek", ec, hf)])
                    g.wait_ge(S("s_sin").h, self.mark[("tables", ec)])
                    g.wait_ge(S("v_qf").h, self.mark[("vqf", ec, hf)])
                    ins = g.tensor_tensor(
                        self.rtmp.ap()[:, 0:HW],
                        self.kraw.ap()[:, 0:HW],
                        self.trig.ap()[:, 2 * T + (ec % 2) * T + hf * HW:2 * T + (ec % 2) * T + hf * HW + HW],
                        ALU.mult)
                    S("g_rope").inc(ins)
                    self.rec(("gk", ec, hf), S("g_rope"))

                @self.V
                def _(v, ec=ec, hf=hf):
                    v.wait_ge(S("g_rope").h, self.mark[("gk", ec, hf)])
                    v.wait_ge(S("v_uk").h, self.mark[("uk", ec, hf)])
                    dst = AP(self.kf, ec * T + hf * HW, [[ND * T, 128], [1, HW]])
                    ins = v.tensor_tensor(dst, dst, self.rtmp.ap()[:, 0:HW],
                                          ALU.add)
                    S("v_kf").inc(ins)
                    self.mark[("gq_prev", nec2, nhf2)] = S("v_kf").n
            self.mark[("uk_last", ec)] = self.S("v_uk").n
            self.mark[("gk_last", ec)] = self.S("g_rope").n
            if ec + 1 < ND:
                self.emit_tables(ec + 1)
        self.rec("qk_done_q", S("v_qf"))
        self.rec("qk_done_k", S("v_kf"))
        self.scA_done = [(nm, S(nm).n) for nm in
                         ["t_q", "t_k", "g_rope", "v_qf", "v_kf", "v_uq", "v_uk",
                          "s_sin", "v_tabf", "s_eq", "s_ek", "t_rq", "t_rk"]]

    def wait_scA(self, eng):
        for nm, val in self.scA_done:
            eng.wait_ge(self.S(nm).h, val)

    def p4(self):
        S, T, NT = self.S, self.T, self.NT

        @self.G
        def _(g):
            d = g.dma_start(out=AP(self.wvsb, 0, [[ND * D, 128], [D, ND], [1, D]]),
                            in_=AP(self.wvT_d, 0, [[D, 128], [128 * D, ND], [1, D]]))
            S("dwv").inc(d, 16)

        for j in range(NT):
            reg = self.SA if j % 2 == 0 else self.SB

            @self.TE
            def _(t, j=j, reg=reg):
                if j == 0:
                    t.wait_ge(S("s_eq").h, self.mark[("eq", ND - 1, max(self.T // 1024, 1) - 1)])
                    t.wait_ge(S("s_ek").h, self.mark[("ek", ND - 1, max(self.T // 1024, 1) - 1)])
                    t.wait_ge(S("v_uk").h, self.mark["qk_done_k"])
                    t.wait_ge(S("v_qf").h, self.mark["qk_done_q"])
                    t.wait_ge(S("v_tabf").h, S("v_tabf").n)
                    t.wait_ge(S("dwv").h, 16)
                if j >= 2:
                    t.wait_ge(S("s_ev").h, self.mark[("v", j - 2)])
                for dc in range(ND):
                    for nb in range(2):
                        ins = t.matmul(
                            reg.ap()[:, nb * 512:nb * 512 + 512],
                            AP(self.hxq, dc * T + j * 128, [[ND * T, 128], [1, 128]]),
                            self.wvsb.ap()[:, dc * D + nb * 512:dc * D + nb * 512 + 512],
                            start=(dc == 0), stop=(dc == ND - 1))
                S("t_v").inc(ins)
                self.rec(("tv", j), S("t_v"))

            @self.SC
            def _(sc, j=j, reg=reg):
                sc.wait_ge(S("t_v").h, self.mark[("tv", j)])
                ins = sc.activation(self.vsb.ap()[:, j * D:j * D + D], reg.ap()[:, 0:1024],
                                    AF.Copy)
                S("s_ev").inc(ins)
                self.rec(("v", j), S("s_ev"))
        self.rec("v_all", S("s_ev"))

    def emit_S(self, j):
        S, T = self.S, self.T
        L = 128 * (j + 1)
        nblk = (L + 511) // 512
        reg = self.SA if j % 2 == 0 else self.SB

        @self.TE
        def _(t, j=j, reg=reg, L=L, nblk=nblk):
            if j <= 1:
                t.wait_ge(S("s_ev").h, self.mark["v_all"])
                t.wait_ge(S("dconst").h, self.mark["const_done"])
            if j >= 2:
                t.wait_ge(S("v_ave").h, self.mark[("ave", j - 2)])
                t.wait_ge(S("v_wte").h, self.mark[("wte_all", j - 2)])
            ndiag = j // 4
            for ec in range(ND):
                for nb in range(nblk):
                    wdt = min(512, L - nb * 512)
                    stop = ec == ND - 1 and nb != ndiag
                    ins = t.matmul(
                        reg.ap()[:, nb * 512:nb * 512 + wdt],
                        AP(self.qf, ec * T + j * 128, [[ND * T, 128], [1, 128]]),
                        AP(self.kf, ec * T + nb * 512, [[ND * T, 128], [1, wdt]]),
                        start=(ec == 0), stop=stop)
            ins = t.matmul(reg.ap()[:, 128 * j:128 * j + 128],
                           self.identb.ap(), self.masksb.ap(),
                           start=False, stop=True)
            S("t_S").inc(ins)
            self.rec(("S", j), S("t_S"))

    def emit_sm(self, j):
        S, T = self.S, self.T
        L = 128 * (j + 1)
        reg = self.SA if j % 2 == 0 else self.SB

        @self.V
        def _(v, j=j, reg=reg, L=L):
            v.wait_ge(S("t_S").h, self.mark[("S", j)])
            ins = v.reduce_max(self.rowmax.ap()[:, j % 2:j % 2 + 1], reg.ap()[:, 0:L],
                               AX.X)
            self.fence(v, ins, "vf")
            ins = v.tensor_scalar(self.negmax.ap()[:, j % 2:j % 2 + 1],
                                  self.rowmax.ap()[:, j % 2:j % 2 + 1], -SM_SCALE, None,
                                  ALU.mult)
            S("v_sm").inc(ins)
            self.rec(("sm", j), S("v_sm"))

        @self.SC
        def _(sc, j=j, reg=reg, L=L):
            sc.wait_ge(S("v_sm").h, self.mark[("sm", j)])
            if j >= 1:
                sc.wait_ge(S("t_wt").h, self.mark[("wt", j - 1, j - 1)])
            ins = sc.activation(self.wsb.ap()[:, 0:L], reg.ap()[:, 0:L], AF.Exp,
                                scale=SM_SCALE,
                                bias=self.negmax.ap()[:, j % 2:j % 2 + 1],
                                accum_out=self.rowsum.ap()[:, j % 2:j % 2 + 1])
            S("s_exp").inc(ins)
            self.rec(("exp", j), S("s_exp"))

    def emit_TAV(self, jj):
        S, T = self.S, self.T
        reg = self.SA if jj % 2 == 0 else self.SB

        @self.TE
        def _(t, jj=jj, reg=reg):
            t.wait_ge(S("s_exp").h, self.mark[("exp", jj)])
            ins = t.transpose(reg.ap()[:, 0:128], self.wsb.ap()[:, 0:128],
                              self.ident.ap())
            S("t_wt").inc(ins)
            self.rec(("wt", jj, 0), S("t_wt"))
            for kt in range(jj + 1):
                if kt + 1 <= jj:
                    if kt + 1 >= 2:
                        t.wait_ge(S("v_wte").h, self.wte_plan[(jj, kt - 1)])
                    slot = (kt + 1) % 2
                    ins = t.transpose(reg.ap()[:, slot * 512:slot * 512 + 128],
                                      self.wsb.ap()[:, (kt + 1) * 128:(kt + 1) * 128 + 128],
                                      self.ident.ap())
                    S("t_wt").inc(ins)
                    self.rec(("wt", jj, kt + 1), S("t_wt"))
                t.wait_ge(S("v_wte").h, self.wte_plan[(jj, kt)])
                for nb in range(2):
                    ins = t.matmul(
                        reg.ap()[:, 1024 + nb * 512:1024 + nb * 512 + 512],
                        self.wtT.ap()[:, (kt % 2) * 128:(kt % 2) * 128 + 128],
                        self.vsb.ap()[:, kt * D + nb * 512:kt * D + nb * 512 + 512],
                        start=(kt == 0), stop=(kt == jj))
            S("t_av").inc(ins)
            self.rec(("av", jj), S("t_av"))

    def emit_TAV_dve(self, jj):
        S, T = self.S, self.T
        reg = self.SA if jj % 2 == 0 else self.SB

        @self.V
        def _(v, jj=jj, reg=reg):
            for kt in range(jj + 1):
                v.wait_ge(S("t_wt").h, self.mark[("wt", jj, kt)])
                slot = kt % 2
                ins = v.tensor_copy(self.wtT.ap()[:, slot * 128:slot * 128 + 128],
                                    reg.ap()[:, slot * 512:slot * 512 + 128])
                S("v_wte").inc(ins)
                assert S("v_wte").n == self.wte_plan[(jj, kt)]
                self.rec(("wte", jj, kt), S("v_wte"))
            self.rec(("wte_all", jj), S("v_wte"))
            v.wait_ge(S("s_exp").h, self.mark[("exp", jj)])
            ins = v.reciprocal(self.recsum.ap()[:, jj % 2:jj % 2 + 1],
                               self.rowsum.ap()[:, jj % 2:jj % 2 + 1])
            self.fence(v, ins, "vf")
            v.wait_ge(S("t_av").h, self.mark[("av", jj)])
            v.wait_ge(S("dx5").h, 16 * (jj + 1))
            if jj >= 2:
                v.wait_ge(S("dx2sp").h, 16 * (jj - 1))
                v.wait_ge(S("s_ap2").h, self.mark[("s_ap2", jj - 2)])
            ins = v.scalar_tensor_tensor(
                self.x2r.ap()[:, (jj % 2) * D:(jj % 2) * D + D],
                reg.ap()[:, 1024:2048],
                self.recsum.ap()[:, jj % 2:jj % 2 + 1],
                self.xr5.ap()[:, (jj % 2) * D:(jj % 2) * D + D],
                ALU.mult, ALU.add)
            S("v_ave").inc(ins)
            self.rec(("ave", jj), S("v_ave"))

    def p5(self):
        S, T, NT = self.S, self.T, self.NT
        self.wte_plan = {}
        n = 0
        for jj in range(NT):
            for kt in range(jj + 1):
                n += 1
                self.wte_plan[(jj, kt)] = n

        @self.SY
        def _(sy):
            sy.wait_ge(S("s_ap1").h, self.NT)
            self.wait_scA(sy)
            d = sy.dma_start(out=self.xr5.ap()[:, 0:D],
                             in_=AP(self.x_d, 0, [[D, 128], [1, D]]))
            S("dx5").inc(d, 16)
            d = sy.dma_start(out=self.xr5.ap()[:, D:2 * D],
                             in_=AP(self.x_d, 128 * D, [[D, 128], [1, D]]))
            S("dx5").inc(d, 16)

        for j in range(NT):
            self.emit_S(j)
            if j >= 1:
                self.emit_TAV(j - 1)
            self.emit_sm(j)
            if j >= 1:
                self.emit_TAV_dve(j - 1)
            if j >= 2:
                self.post_tile(j - 2)
        self.emit_TAV(NT - 1)
        self.emit_TAV_dve(NT - 1)
        if NT >= 2:
            self.post_tile(NT - 2)
        self.post_tile(NT - 1)

    def post_tile(self, jj):
        S, T, NT = self.S, self.T, self.NT
        xs = (lambda e, jj=jj: self.x2r.ap()[:, (jj % 2) * D:(jj % 2) * D + D])
        dst = (lambda e, jj=jj: self.xn_tm.ap()[:, jj * D:jj * D + D])
        self.ln_tile(xs, S("v_ave"), self.mark[("ave", jj)], jj, dst,
                     (jj % 2) * 8, "v_var2", "s_ap2")

        @self.SY
        def _(sy, jj=jj):
            if jj + 2 < NT:
                sy.wait_ge(S("v_ave").h, self.mark[("ave", jj)])
                d = sy.dma_start(out=self.xr5.ap()[:, ((jj + 2) % 2) * D:((jj + 2) % 2) * D + D],
                                 in_=AP(self.x_d, (jj + 2) * 128 * D, [[D, 128], [1, D]]))
                S("dx5").inc(d, 16)
            sy.wait_ge(S("s_ap2").h, self.mark[("s_ap2", jj)])
            d = sy.dma_start(out=AP(self.x2sp_d, jj * 128 * D, [[D, 128], [1, D]]),
                             in_=self.x2r.ap()[:, (jj % 2) * D:(jj % 2) * D + D])
            S("dx2sp").inc(d, 16)

        @self.V
        def _(v, jj=jj):
            v.wait_ge(S("s_ap2").h, self.mark[("s_ap2", jj)])
            if jj >= 1:
                v.wait_ge(S("g_xmax").h, self.mark[("xmax", jj - 1)])
            ins = v.reduce_max(self.xmc.ap()[:, 1:2],
                               self.xn_tm.ap()[:, jj * D:jj * D + D],
                               AX.X, apply_absolute_value=True)
            self.fence(v, ins, "vf")
            if jj == 0:
                ins = v.tensor_copy(self.xmc.ap()[:, 0:1], self.xmc.ap()[:, 1:2])
            else:
                ins = v.tensor_tensor(self.xmc.ap()[:, 0:1], self.xmc.ap()[:, 0:1],
                                      self.xmc.ap()[:, 1:2], ALU.max)
            S("g_xmax").inc(ins)
        self.rec(("xmax", jj), S("g_xmax"))

    def p6_local(self):
        S = self.S

        @self.TE
        def _(t):
            t.wait_ge(S("g_xmax").h, self.mark[("xmax", self.NT - 1)])
            t.wait_ge(S("v_ave").h, self.mark[("ave", self.NT - 1)])
            t.wait_ge(S("v_wte").h, self.mark[("wte_all", self.NT - 1)])
            ins = t.transpose(self.SA.ap()[0:1, 0:128], self.xmc.ap()[:, 0:1],
                              self.ident.ap())
            S("t_pmax").inc(ins)

        @self.V
        def _(v):
            v.wait_ge(S("t_pmax").h, 1)
            ins = v.reduce_max(self.gmx.ap()[:, 0:1], self.SA.ap()[0:1, 0:128], AX.X)
            S("v_pmax").inc(ins)

    def p6_cc(self):
        S = self.S

        @self.G
        def _(g):
            g.wait_ge(S("v_pmax").h, 1)
            d = g.dma_start(out=self.cc1i_d.ap(), in_=self.gmx.ap()[:, 0:1])
            S("dcc").inc(d, 16)
            g.wait_ge(S("dcc").h, S("dcc").n)
            g.collective_compute("AllReduce", ALU.max,
                                 replica_groups=self.rg,
                                 ins=[self.cc1i_d.ap()], outs=[self.cc1o_d.ap()])
            d = g.dma_start(out=self.gmx.ap()[:, 2:3], in_=self.cc1o_d.ap())
            S("dcc").inc(d, 16)
            g.wait_ge(S("dcc").h, S("dcc").n)
            ins = g.sem_inc(S("g_cc1").h, 1)
            S("g_cc1").n += 1

    def p6_gather_in(self):
        S = self.S

        @self.SY
        def _(sy):
            d = sy.dma_start(out=self.gm8.ap()[:, 0:8], in_=self.gm8a_d.ap())
            S("dcc").inc(d, 16)
        n = S("dcc").n

        @self.V
        def _(v, n=n):
            v.wait_ge(S("dcc").h, n)
            ins = v.reduce_max(self.gmx.ap()[:, 2:3], self.gm8.ap()[:, 0:8], AX.X)
            S("g_cc1").inc(ins)

    def p6_scales(self):
        S = self.S

        @self.V
        def _(v):
            v.wait_ge(S("g_cc1").h, 1)
            v.wait_ge(S("s_sig").h, 1)
            ins = v.tensor_tensor(self.rsx.ap()[:, 4:5], self.sgc.ap()[:, 0:1],
                                  self.gmx.ap()[:, 2:3], ALU.mult)
            self.fence(v, ins, "vf")
            ins = v.reciprocal(self.rsx.ap()[:, 5:6], self.rsx.ap()[:, 4:5])
            self.fence(v, ins, "vf")
            v.tensor_scalar(self.rsx.ap()[:, 0:1], self.rsx.ap()[:, 5:6], 127.0,
                            None, ALU.mult)
            ins = v.tensor_scalar(self.rsx.ap()[:, 1:2], self.rsx.ap()[:, 4:5],
                                  1.0 / 127.0, None, ALU.mult)
            S("v_rsx").inc(ins)

        @self.G
        def _(g):
            g.wait_ge(S("v_rsx").h, 1)
            d = g.dma_start(out=self.rsxb.ap()[:, 0:1],
                            in_=AP(self.rsx, 0, [[8, 1], [0, 128], [1, 1]]))
            S("dbc").inc(d, 16)
            d = g.dma_start(out=self.rsxb.ap()[:, 1:2],
                            in_=AP(self.rsx, 1, [[8, 1], [0, 128], [1, 1]]))
            S("dbc").inc(d, 16)
        self.rec("rsx_b", S("dbc"))

    def emit_xq(self, h):
        S, T, NT = self.S, self.T, self.NT
        NH = max(T // 1024, 1)
        HW = T // NH
        TPH = NT // NH
        for dc in range(ND):
            @self.V
            def _(v, dc=dc, h=h):
                v.wait_ge(S("v_xne").h, 2 * (h + 1) * TPH)
                v.wait_ge(S("dbc").h, self.mark["rsx_b"])
                src = AP(self.hxq, dc * T + h * HW, [[ND * T, 128], [1, HW]])
                ins = v.tensor_tensor(self.i327.ap()[:, 0:HW], src,
                                      AP(self.rsxb, 0, [[2, 128], [0, HW]]),
                                      ALU.mult)
                self.fence(v, ins, "vf")
                ins = v.tensor_scalar(src, self.i327.ap()[:, 0:HW], 127.0,
                                      -128.0, ALU.min, ALU.max)
                S("v_quant").inc(ins)
        self.rec(("xq", h), S("v_quant"))

    def p7(self):
        S, T, NT = self.S, self.T, self.NT
        NH = max(T // 1024, 1)
        HW = T // NH
        TPH = NT // NH

        for j in range(NT):
            @self.TE
            def _(t, j=j):
                if self.mode == "L2":
                    t.wait_ge(S("dxn").h, self.mark[("xn_in", j)])
                else:
                    t.wait_ge(S("s_ap2").h, self.mark[("s_ap2", j)])
                    if j == 0:
                        t.wait_ge(S("v_ave").h, self.mark[("ave", NT - 1)])
                        t.wait_ge(S("v_wte").h, self.mark[("wte_all", NT - 1)])
                        t.wait_ge(S("dx2sp").h, 16 * NT)
                        t.wait_ge(S("v_pmax").h, 1)
                for grp in range(2):
                    if 2 * j + grp >= 2:
                        t.wait_ge(S("v_xne").h, 2 * j + grp - 1)
                    pt = self.PT0 if (2 * j + grp) % 2 == 0 else self.PT1
                    for q in range(4):
                        dc = grp * 4 + q
                        ins = t.transpose(
                            pt.ap()[:, q * 128:q * 128 + 128],
                            self.xn_tm.ap()[:, j * D + dc * 128:j * D + dc * 128 + 128],
                            self.identb.ap())
                    S("t_xnT").inc(ins)

            @self.SC
            def _(sc, j=j):
                for grp in range(2):
                    sc.wait_ge(S("t_xnT").h, 2 * j + grp + 1)
                    pt = self.PT0 if (2 * j + grp) % 2 == 0 else self.PT1
                    ins = sc.activation(
                        AP(self.hxq, (4 * grp) * T + j * 128,
                           [[ND * T, 128], [T, 4], [1, 128]]),
                        pt.ap()[:, 0:512], AF.Copy)
                    S("v_xne").inc(ins)
            if (j + 1) % TPH == 0:
                h = (j + 1) // TPH - 1
                if not (self.mode == "L2" and NH >= 2 and h == NH - 1):
                    self.emit_xq(h)
        self.rec("xn_fm_done", S("v_xne"))

    def p8_chunk_front_scale(self, c):
        S, T = self.S, self.T
        NHF2 = max(T // 1024, 1)

        @self.SY
        def _(sy, c=c):
            if c >= 3:
                sy.wait_ge(S("v_wq").h, self.mark[("wq", c - 3)])
            if c >= 2:
                sy.wait_ge(S("v_scg").h, self.mark[("scg", c - 2)])
            for wt_, dst in [(self.wgT_d, self.wgf), (self.wuT_d, self.wuf)]:
                d = sy.dma_start(
                    out=AP(dst, (c % 3) * 1024, [[3 * 1024, 128], [1, 1024]]),
                    in_=AP(wt_, c * 128 * 1024, [[1024, 128], [1, 1024]]))
                S("dwf").inc(d, 16)
            d = sy.dma_start(
                out=AP(self.wdf, (c % 3) * 1024, [[3 * 1024, 128], [1, 1024]]),
                in_=AP(self.wdT_d, c * 128 * D, [[D, 128], [1, D]]))
            S("dwf").inc(d, 16)
            for wn_, dst in [(self.wgN_d, self.wgn), (self.wuN_d, self.wun)]:
                d = sy.dma_start(
                    out=AP(dst, (c % 2) * 1024, [[2 * 1024, 128], [1, 1024]]),
                    in_=AP(wn_, c * 128 * D, [[D, 128], [1, D]]))
                S("dwn2").inc(d, 16)
            self.rec(("wload", c), S("dwf"))
            self.rec(("wnload", c), S("dwn2"))

        @self.V
        def _(v, c=c):
            v.wait_ge(S("dwn2").h, self.mark[("wnload", c)])
            v.wait_ge(S("dbc").h, self.mark["rsx_b"])
            if c >= 4:
                if c - 4 >= 4:
                    v.wait_ge(S("t_rcT").h, self.mark[("rcT", c - 4)])
                else:
                    v.wait_ge(S("dbc").h, self.mark[("rb", c - 4)])
            if c >= 8:
                v.wait_ge(S("s_silu").h, (c - 7) * NHF2)
                v.wait_ge(S("v_hmul").h, (c - 7) * NHF2)
            for src, mx, rc in [(self.wgn, self.swg, self.rcg),
                                (self.wun, self.swu, self.rcu)]:
                ins = v.reduce_max(mx.ap()[:, c % 4:c % 4 + 1],
                                   src.ap()[:, (c % 2) * 1024:(c % 2) * 1024 + 1024],
                                   AX.X, apply_absolute_value=True)
                self.fence(v, ins, "vf")
                ins = v.reciprocal(rc.ap()[:, c % 4:c % 4 + 1],
                                   mx.ap()[:, c % 4:c % 4 + 1])
                self.fence(v, ins, "vf")
                v.tensor_scalar(rc.ap()[:, c % 4:c % 4 + 1],
                                rc.ap()[:, c % 4:c % 4 + 1], 127.0, None,
                                ALU.mult)
            for mx, scc in [(self.swg, self.scg), (self.swu, self.scu)]:
                ins = v.tensor_tensor(scc.ap()[:, c % 8:c % 8 + 1],
                                      mx.ap()[:, c % 4:c % 4 + 1],
                                      self.rsxb.ap()[:, 1:2], ALU.mult)
                self.fence(v, ins, "vf")
                v.tensor_scalar(scc.ap()[:, c % 8:c % 8 + 1],
                                scc.ap()[:, c % 8:c % 8 + 1], 1.0 / 127.0, None,
                                ALU.mult)
            ins = v.engine_nop()
            S("v_scg").inc(ins)
            self.rec(("scg", c), S("v_scg"))

        if c < 4:
            @self.G
            def _(g, c=c):
                g.wait_ge(S("v_scg").h, self.mark[("scg", c)])
                d = g.dma_start(out=AP(self.rgrow, (c % 4) * 128,
                                       [[4 * 128, 1], [1, 128]]),
                                in_=self.rcg.ap()[:, c % 4:c % 4 + 1])
                S("dbc").inc(d, 16)
                d = g.dma_start(out=AP(self.rurow, (c % 4) * 128,
                                       [[4 * 128, 1], [1, 128]]),
                                in_=self.rcu.ap()[:, c % 4:c % 4 + 1])
                S("dbc").inc(d, 16)
                g.wait_ge(S("dbc").h, S("dbc").n)
                d = g.dma_start(out=self.rbg.ap()[:, (c % 4) * 128:(c % 4) * 128 + 128],
                                in_=AP(self.rgrow, (c % 4) * 128,
                                       [[4 * 128, 1], [0, 128], [1, 128]]))
                S("dbc").inc(d, 16)
                d = g.dma_start(out=self.rbu.ap()[:, (c % 4) * 128:(c % 4) * 128 + 128],
                                in_=AP(self.rurow, (c % 4) * 128,
                                       [[4 * 128, 1], [0, 128], [1, 128]]))
                S("dbc").inc(d, 16)
                self.rec(("rb", c), S("dbc"))
    def p8_chunk_front_T(self, c):
        S = self.S

        NHF2 = max(self.T // 1024, 1)

        @self.TE
        def _(t, c=c):
            t.wait_ge(S("v_scg").h, self.mark[("scg", c)])
            t.wait_ge(S("s_silu").h, self.mark.get(("silu", c - 5, NHF2 - 1), 0))
            t.transpose(self.GB.ap()[0:1, 0:128],
                        self.rcg.ap()[:, c % 4:c % 4 + 1], self.ident.ap())
            ins = t.transpose(self.GB.ap()[0:1, 128:256],
                              self.rcu.ap()[:, c % 4:c % 4 + 1], self.ident.ap())
            S("t_rcT").inc(ins)
            self.rec(("rcT", c), S("t_rcT"))

    def p8_chunk_front_rcp(self, c):
        S = self.S

        @self.V
        def _(v, c=c):
            v.wait_ge(S("t_rcT").h, self.mark[("rcT", c)])
            if c >= 8:
                v.wait_ge(S("dbc").h, self.mark[("rb", c - 4)])
            v.tensor_copy(AP(self.rgrow, (c % 4) * 128, [[4 * 128, 1], [1, 128]]),
                          self.GB.ap()[0:1, 0:128])
            ins = v.tensor_copy(AP(self.rurow, (c % 4) * 128,
                                   [[4 * 128, 1], [1, 128]]),
                                self.GB.ap()[0:1, 128:256])
            S("v_rcp").inc(ins)
            self.rec(("rcp", c), S("v_rcp"))

        @self.G
        def _(g, c=c):
            g.wait_ge(S("v_rcp").h, self.mark[("rcp", c)])
            d = g.dma_start(out=self.rbg.ap()[:, (c % 4) * 128:(c % 4) * 128 + 128],
                            in_=AP(self.rgrow, (c % 4) * 128,
                                   [[4 * 128, 1], [0, 128], [1, 128]]))
            S("dbc").inc(d, 16)
            d = g.dma_start(out=self.rbu.ap()[:, (c % 4) * 128:(c % 4) * 128 + 128],
                            in_=AP(self.rurow, (c % 4) * 128,
                                   [[4 * 128, 1], [0, 128], [1, 128]]))
            S("dbc").inc(d, 16)
            self.rec(("rb", c), S("dbc"))

    def p8_chunk_front_b(self, c):
        S, T = self.S, self.T
        NHF2 = max(T // 1024, 1)

        @self.V
        def _(v, c=c):
            v.wait_ge(S("dbc").h, self.mark[("rb", c)])
            v.wait_ge(S("dwf").h, self.mark[("wload", c)])
            if c == 0 and self.mode != "L2":
                v.wait_ge(S("dbc").h, self.mark["rwdb_done"])
            if c >= 5:
                v.wait_ge(S("t_gu").h, self.mark[("gu", c - 5, NHF2 - 1)])
            for src, rb, dint in [(self.wgf, self.rbg, self.wgi),
                                  (self.wuf, self.rbu, self.wui)]:
                ins = v.tensor_tensor(
                    self.i328.ap(),
                    AP(src, (c % 3) * 1024, [[3 * 1024, 128], [128, 8], [1, 128]]),
                    AP(rb, (c % 4) * 128, [[4 * 128, 128], [0, 8], [1, 128]]),
                    ALU.mult)
                self.fence(v, ins, "vf")
                ins = v.tensor_copy(dint.ap()[:, (c % 5) * 1024:(c % 5) * 1024 + 1024],
                                    self.i328.ap())
                self.fence(v, ins, "vf")
            if self.mode == "L2" and c >= 2:
                v.wait_ge(S("dwsp").h, 16 * (c - 1))
            ins = v.tensor_tensor(self.i328.ap(),
                                  AP(self.wdf, (c % 3) * 1024, [[3 * 1024, 128], [1, 1024]]),
                                  self.rwdb.ap(), ALU.mult)
            self.fence(v, ins, "vf")
            wds = (c % 2) * 1024 if self.mode == "L2" else c * 1024
            ins = v.tensor_copy(self.wdi.ap()[:, wds:wds + 1024], self.i328.ap())
            S("v_wq").inc(ins)
            self.rec(("wq", c), S("v_wq"))

        if self.mode == "L2":
            @self.G
            def _(g, c=c):
                g.wait_ge(S("v_wq").h, self.mark[("wq", c)])
                d = g.dma_start(
                    out=AP(self.wdi_d, c * 1024, [[NI * 1024, 128], [1, 1024]]),
                    in_=self.wdi.ap()[:, (c % 2) * 1024:(c % 2) * 1024 + 1024])
                S("dwsp").inc(d, 16)

    def p8_chunk_back_hf(self, c, hf):
        self.p8_chunk_back_hf_te(c, hf)
        self.p8_chunk_back_hf_v(c, hf)

    def p8_chunk_back_hf_te(self, c, hf):
        S, T = self.S, self.T
        NHF2 = max(T // 1024, 1)
        HW2 = min(T, 1024)
        GP = self.GA if hf == 0 else self.GB
        UP = self.UA if hf == 0 else self.UB

        @self.TE
        def _(t, c=c, hf=hf, GP=GP, UP=UP):
            t.wait_ge(S("v_wq").h, self.mark[("wq", c)])
            if c == 0:
                t.wait_ge(S("v_quant").h, self.mark[("xq", hf)])
                t.wait_ge(S("v_xne").h, self.mark["xn_fm_done"])
            if c >= 1:
                t.wait_ge(S("s_silu").h, self.mark[("silu", c - 1, hf)])
                t.wait_ge(S("v_hmul").h, self.mark[("hmul", c - 1, hf)])
            if NHF2 >= 2 and hf == NHF2 - 1 and 4 <= c + 4 < NI:
                t.wait_ge(S("v_rcp").h, self.mark[("rcp", c + 4)])
            for dint, P_ in [(self.wgi, GP), (self.wui, UP)]:
                for dc in range(ND):
                    for nb in range(HW2 // 512):
                        ins = t.matmul(
                            P_.ap()[:, nb * 512:nb * 512 + 512],
                            dint.ap()[:, (c % 5) * 1024 + dc * 128:(c % 5) * 1024 + dc * 128 + 128],
                            AP(self.hxq, dc * T + hf * 1024 + nb * 512,
                               [[ND * T, 128], [1, 512]]),
                            start=(dc == 0), stop=(dc == ND - 1))
            S("t_gu").inc(ins)
            self.rec(("gu", c, hf), S("t_gu"))

        @self.SC
        def _(sc, c=c, hf=hf, GP=GP):
            sc.wait_ge(S("t_gu").h, self.mark[("gu", c, hf)])
            sc.wait_ge(S("v_scg").h, self.mark[("scg", c)])
            if c >= 1:
                sc.wait_ge(S("v_hmul").h, self.mark.get(("hmul", c - 1, hf), 0))
            ins = sc.activation(self.silu_sb.ap()[:, hf * 1024:hf * 1024 + HW2],
                                GP.ap()[:, 0:HW2], AF.Silu,
                                scale=self.scg.ap()[:, c % 8:c % 8 + 1])
            S("s_silu").inc(ins)
            self.rec(("silu", c, hf), S("s_silu"))

    def p8_chunk_back_hf_v(self, c, hf):
        S, T = self.S, self.T
        HW2 = min(T, 1024)
        UP = self.UA if hf == 0 else self.UB

        @self.V
        def _(v, c=c, hf=hf, UP=UP):
            v.wait_ge(S("s_silu").h, self.mark[("silu", c, hf)])
            if c >= 2:
                v.wait_ge(S("dhsp").h, 16 * (c - 1))
                v.wait_ge(S("g_hmax").h, self.mark.get(("hmax", c - 2, hf), 0))
            ins = v.scalar_tensor_tensor(
                self.hst.ap()[:, (c % 2) * self.T + hf * 1024:(c % 2) * self.T + hf * 1024 + HW2],
                UP.ap()[:, 0:HW2],
                self.scu.ap()[:, c % 8:c % 8 + 1],
                self.silu_sb.ap()[:, hf * 1024:hf * 1024 + HW2],
                ALU.mult, ALU.mult)
            S("v_hmul").inc(ins)
            self.rec(("hmul", c, hf), S("v_hmul"))

        @self.V
        def _(v, c=c, hf=hf):
            v.wait_ge(S("v_hmul").h, self.mark[("hmul", c, hf)])
            if (c, hf) != (0, 0):
                v.wait_ge(S("g_hmax").h, S("g_hmax").n)
            ins = v.reduce_max(self.xmc.ap()[:, 3:4],
                               self.hst.ap()[:, (c % 2) * self.T + hf * 1024:(c % 2) * self.T + hf * 1024 + HW2],
                               AX.X, apply_absolute_value=True)
            self.fence(v, ins, "vf")
            if c == 0 and hf == 0:
                ins = v.tensor_copy(self.xmc.ap()[:, 2:3], self.xmc.ap()[:, 3:4])
            else:
                ins = v.tensor_tensor(self.xmc.ap()[:, 2:3], self.xmc.ap()[:, 2:3],
                                      self.xmc.ap()[:, 3:4], ALU.max)
            S("g_hmax").inc(ins)
            self.rec(("hmax", c, hf), S("g_hmax"))

    def p8_chunk_spill(self, c):
        S, T = self.S, self.T

        @self.SC
        def _(sc, c=c):
            sc.wait_ge(S("g_hmax").h, self.mark[("hmax", c, max(self.T // 1024, 1) - 1)])
            d = sc.dma_start(out=AP(self.hsp_d, c * 128 * T, [[T, 128], [1, T]]),
                             in_=self.hst.ap()[:, (c % 2) * T:(c % 2) * T + T])
            S("dhsp").inc(d, 16)

    def p8_head(self):
        S = self.S

        @self.V
        def _(v):
            v.wait_ge(S("dmr").h, 16)
            ins = v.reciprocal(self.Mrow.ap(), self.Mrow.ap())
            self.fence(v, ins, "vf")
            ins = v.tensor_scalar(self.Mrow.ap(), self.Mrow.ap(), 127.0, None,
                                  ALU.mult)
            S("v_rwd").inc(ins)
        n4 = min(4, NI)
        for cc in range(min(3, NI)):
            self.p8_chunk_front_scale(cc)
        if NI > 0:
            self.p8_chunk_front_b(0)
        if n4 == 4:
            self.p8_chunk_front_scale(3)
        for cc in range(1, n4):
            self.p8_chunk_front_b(cc)
        self._p8_head_done = True

    def p8(self):
        S = self.S
        Q = 4

        if self.mode == "full":
            for hc in range(ND):
                @self.G
                def _(g, hc=hc):
                    g.wait_ge(S("v_wdm").h, self.mark[("wdm", hc)])
                    d = g.dma_start(out=AP(self.Mrow, hc * 128, [[D, 1], [1, 128]]),
                                    in_=self.swd.ap()[:, hc:hc + 1])
                    S("dbc").inc(d, 16)
            self.rec("mrow_done", S("dbc"))

            @self.V
            def _(v):
                v.wait_ge(S("dbc").h, self.mark["mrow_done"])
                ins = v.reciprocal(self.rwdrow.ap(), self.Mrow.ap())
                self.fence(v, ins, "vf")
                ins = v.tensor_scalar(self.rwdrow.ap(), self.rwdrow.ap(), 127.0,
                                      None, ALU.mult)
                S("v_rwd").inc(ins)

            @self.G
            def _(g):
                g.wait_ge(S("v_rwd").h, 1)
                d = g.dma_start(out=self.rwdb.ap(),
                                in_=AP(self.rwdrow, 0, [[D, 1], [0, 128], [1, D]]))
                S("dbc").inc(d, 16)
            self.rec("rwdb_done", S("dbc"))

        start = Q if getattr(self, "_p8_head_done", False) else 0
        NHF2 = max(self.T // 1024, 1)
        for cc in range(start, NI + Q + 1):
            cm = cc - Q
            if cc < NI:
                self.p8_chunk_front_scale(cc)
            if 0 <= cm < NI:
                self.p8_chunk_back_hf_te(cm, 0)
            if 4 <= cc < NI:
                self.p8_chunk_front_T(cc)
                self.p8_chunk_front_rcp(cc)
            if 4 <= cc - 1 < NI:
                self.p8_chunk_front_b(cc - 1)
            if self.mode == "L2" and NHF2 >= 2 and cc == start:
                self.emit_xq(NHF2 - 1)
            if 0 <= cm < NI:
                self.p8_chunk_back_hf_v(cm, 0)
                for hf in range(1, NHF2):
                    self.p8_chunk_back_hf_te(cm, hf)
                    self.p8_chunk_back_hf_v(cm, hf)
                self.p8_chunk_spill(cm)
        self.rec("h_all", S("g_hmax"))

    def p9_local(self):
        S = self.S

        @self.TE
        def _(t):
            nhf2 = max(self.T // 1024, 1)
            t.wait_ge(S("g_hmax").h, self.mark["h_all"])
            t.wait_ge(S("s_silu").h, self.mark[("silu", NI - 1, nhf2 - 1)])
            t.wait_ge(S("v_hmul").h, self.mark[("hmul", NI - 1, nhf2 - 1)])
            ins = t.transpose(self.GA.ap()[0:1, 0:128], self.xmc.ap()[:, 2:3],
                              self.ident.ap())
            S("t_pmax").inc(ins)
        npm = S("t_pmax").n

        @self.V
        def _(v, npm=npm):
            v.wait_ge(S("t_pmax").h, npm)
            ins = v.reduce_max(self.gmx.ap()[:, 1:2], self.GA.ap()[0:1, 0:128], AX.X)
            S("v_pmax").inc(ins)
        self.rec("pmax2", S("v_pmax"))

    def p9_cc(self):
        S = self.S

        @self.G
        def _(g):
            g.wait_ge(S("v_pmax").h, self.mark["pmax2"])
            d = g.dma_start(out=self.cc2i_d.ap(), in_=self.gmx.ap()[:, 1:2])
            S("dcc").inc(d, 16)
            g.wait_ge(S("dcc").h, S("dcc").n)
            g.collective_compute("AllReduce", ALU.max,
                                 replica_groups=self.rg,
                                 ins=[self.cc2i_d.ap()], outs=[self.cc2o_d.ap()])
            d = g.dma_start(out=self.gmx.ap()[:, 3:4], in_=self.cc2o_d.ap())
            S("dcc").inc(d, 16)
            g.wait_ge(S("dcc").h, S("dcc").n)
            ins = g.sem_inc(S("g_cc2").h, 1)
            S("g_cc2").n += 1

    def p9_gather_in(self):
        S = self.S

        @self.SY
        def _(sy):
            d = sy.dma_start(out=self.gm8.ap()[:, 0:8], in_=self.gm8b_d.ap())
            S("dcc").inc(d, 16)
        n = S("dcc").n

        @self.V
        def _(v, n=n):
            v.wait_ge(S("dcc").h, n)
            ins = v.reduce_max(self.gmx.ap()[:, 3:4], self.gm8.ap()[:, 0:8], AX.X)
            S("g_cc2").inc(ins)

    def p9_scales(self):
        S = self.S

        @self.V
        def _(v):
            v.wait_ge(S("g_cc2").h, 1)
            v.wait_ge(S("s_sig").h, 1)
            if self.mode == "L3":
                v.wait_ge(S("dmr").h, self.mark["mrow_in"])
            ins = v.tensor_tensor(self.rsx.ap()[:, 6:7], self.sgc.ap()[:, 1:2],
                                  self.gmx.ap()[:, 3:4], ALU.mult)
            self.fence(v, ins, "vf")
            ins = v.reciprocal(self.rsx.ap()[:, 7:8], self.rsx.ap()[:, 6:7])
            self.fence(v, ins, "vf")
            v.tensor_scalar(self.rsx.ap()[:, 2:3], self.rsx.ap()[:, 7:8], 127.0,
                            None, ALU.mult)
            ins = v.tensor_scalar(self.rsx.ap()[:, 3:4], self.rsx.ap()[:, 6:7],
                                  1.0 / 127.0, None, ALU.mult)
            self.fence(v, ins, "vf")
            ins = v.tensor_scalar(self.rsx.ap()[:, 4:5], self.rsx.ap()[:, 3:4],
                                  1.0 / 127.0, None, ALU.mult)
            self.fence(v, ins, "vf")
            ins = v.tensor_scalar(self.swdhrow.ap(), self.Mrow.ap(),
                                  self.rsx.ap()[:, 4:5], None, ALU.mult)
            S("v_rsh").inc(ins)

        @self.G
        def _(g):
            g.wait_ge(S("v_rsh").h, 1)
            d = g.dma_start(out=self.rshb.ap(),
                            in_=AP(self.rsx, 2, [[8, 1], [0, 128], [1, 1]]))
            S("dbc").inc(d, 16)
            d = g.dma_start(out=self.swdhb.ap(),
                            in_=AP(self.swdhrow, 0, [[D, 1], [0, 128], [1, D]]))
            S("dbc").inc(d, 16)
        self.rec("rsh_b", S("dbc"))

    def p10(self):
        S, T = self.S, self.T
        for c in range(NI):
            @self.SY
            def _(sy, c=c):
                if self.mode != "L3":
                    sy.wait_ge(S("dhsp").h, 16 * (c + 1))
                if c == 0 and self.mode != "L3":
                    sy.wait_ge(S("t_gu").h, self.mark[("gu", NI - 1, max(self.T // 1024, 1) - 1)])
                    sy.wait_ge(S("g_hmax").h, self.mark["h_all"])
                    sy.wait_ge(S("v_hmul").h, S("v_hmul").n)
                    sy.wait_ge(S("dbc").h, self.mark["rsh_b"])
                if c >= 2:
                    sy.wait_ge(S("v_hq").h, self.mark[("hq", c - 2)])
                d = sy.dma_start(out=self.hld.ap()[:, (c % 2) * T:(c % 2) * T + T],
                                 in_=AP(self.hsp_d, c * 128 * T, [[T, 128], [1, T]]))
                S("dhld").inc(d, 16)
                self.rec(("hldd", c), S("dhld"))

            @self.V
            def _(v, c=c):
                v.wait_ge(S("dhld").h, self.mark[("hldd", c)])
                v.wait_ge(S("dbc").h, self.mark["rsh_b"])
                for hh2 in range(T // 1024 if T >= 1024 else 1):
                    W2 = min(T, 1024)
                    sl = self.hld.ap()[:, (c % 2) * T + hh2 * W2:(c % 2) * T + hh2 * W2 + W2]
                    ins = v.tensor_tensor(self.i32F.ap()[:, 0:W2], sl,
                                          AP(self.rshb, 0, [[1, 128], [0, W2]]), ALU.mult)
                    self.fence(v, ins, "vf")
                    ins = v.tensor_scalar(
                        AP(self.hq, c * T + hh2 * W2, [[NI * T, 128], [1, W2]]),
                        self.i32F.ap()[:, 0:W2], 127.0, -128.0, ALU.min, ALU.max)
                S("v_hq").inc(ins)
                self.rec(("hq", c), S("v_hq"))
        self.rec("hq_all", S("v_hq"))

    def p11(self):
        S, T, NT = self.S, self.T, self.NT

        @self.SY
        def _(sy):
            for st in range(min(2, NT)):
                d = sy.dma_start(out=self.x2l.ap()[:, (st % 2) * D:(st % 2) * D + D],
                                 in_=AP(self.x2sp_d, st * 128 * D, [[D, 128], [1, D]]))
                S("dx2ld").inc(d, 16)

        for st in range(NT):
            DP = self.DPA if st % 2 == 0 else self.DPB

            @self.TE
            def _(t, st=st, DP=DP):
                if st <= 1 and self.mode != "L3":
                    nhf2 = max(self.T // 1024, 1)
                    t.wait_ge(S("s_silu").h, self.mark[("silu", NI - 1, nhf2 - 1)])
                    t.wait_ge(S("s_upe").h, self.mark[("upe", NI - 1, nhf2 - 1)])

                if st >= 2:
                    t.wait_ge(S("v_dne").h, self.mark[("dne", st - 2)])
                for c in range(NI):
                    if self.mark[("hq", c)] > 0:
                        t.wait_ge(S("v_hq").h, self.mark[("hq", c)])
                    if self.mode == "L3" and st <= 1:
                        t.wait_ge(S("dwf").h, self.mark[("wdiL", c)])
                    for nb in range(2):
                        ins = t.matmul(
                            DP.ap()[:, nb * 512:nb * 512 + 512],
                            AP(self.hq, c * T + st * 128, [[NI * T, 128], [1, 128]]),
                            self.wdi.ap()[:, c * 1024 + nb * 512:c * 1024 + nb * 512 + 512],
                            start=(c == 0), stop=(c == NI - 1))
                S("t_dn").inc(ins)
                self.rec(("dn", st), S("t_dn"))

            @self.V
            def _(v, st=st, DP=DP):
                v.wait_ge(S("t_dn").h, self.mark[("dn", st)])
                v.wait_ge(S("dx2ld").h, 16 * (st + 1))
                v.wait_ge(S("dbc").h, self.mark["rsh_b"])
                if st >= 2:
                    v.wait_ge(S("dout").h, 16 * (st - 1))
                ins = v.tensor_tensor(self.dnt.ap(), DP.ap(), self.swdhb.ap(), ALU.mult)
                self.fence(v, ins, "vf")
                ins = v.tensor_add(self.outr.ap()[:, (st % 2) * D:(st % 2) * D + D],
                                   self.dnt.ap(),
                                   self.x2l.ap()[:, (st % 2) * D:(st % 2) * D + D])
                S("v_dne").inc(ins)
                self.rec(("dne", st), S("v_dne"))

            @self.SY
            def _(sy, st=st):
                sy.wait_ge(S("v_dne").h, self.mark[("dne", st)])
                d = sy.dma_start(out=AP(self.out_d, st * 128 * D, [[D, 128], [1, D]]),
                                 in_=self.outr.ap()[:, (st % 2) * D:(st % 2) * D + D])
                S("dout").inc(d, 16)
                if st + 2 < NT:
                    d = sy.dma_start(
                        out=self.x2l.ap()[:, ((st + 2) % 2) * D:((st + 2) % 2) * D + D],
                        in_=AP(self.x2sp_d, (st + 2) * 128 * D, [[D, 128], [1, D]]))
                    S("dx2ld").inc(d, 16)

        @self.SY
        def _(sy):
            sy.wait_ge(S("dout").h, 16 * NT)


    def pL1_out(self):
        S, NT = self.S, self.NT

        @self.G
        def _(g):
            g.wait_ge(S("s_ap2").h, self.mark[("s_ap2", NT - 1)])
            d = g.dma_start(out=self.xnt_d.ap(), in_=self.xn_tm.ap())
            S("dtap").inc(d, 16)
            g.wait_ge(S("v_pmax").h, 1)
            d = g.dma_start(out=self.gm1_d.ap(), in_=self.gmx.ap()[:, 0:1])
            S("dtap").inc(d, 16)
            g.wait_ge(S("v_wdm").h, self.mark[("wdm", ND - 1)])
            d = g.dma_start(out=AP(self.mrow_d, 0, [[1, 128], [128, ND]]),
                            in_=self.swd.ap()[:, 0:ND],
                            allow_slow_non_contiguous=True)
            S("dtap").inc(d, 16)
            g.wait_ge(S("dtap").h, S("dtap").n)

        @self.SY
        def _(sy):
            sy.wait_ge(S("dx2sp").h, 16 * NT)

    def pL2_in(self):
        S, NT = self.S, self.NT

        @self.SY
        def _(sy):
            d = sy.dma_start(out=self.Mrow.ap(), in_=self.mrow_d.ap())
            S("dmr").inc(d, 16)
            sy.wait_ge(S("v_rwd").h, 1)
            d = sy.dma_start(out=self.rwdb.ap(),
                             in_=AP(self.Mrow, 0, [[D, 1], [0, 128], [1, D]]))
            S("dbs").inc(d, 16)
            self.rec("rwdb_done", S("dbs"))

        @self.SC
        def _(sc):
            for j in range(NT):
                d = sc.dma_start(out=self.xn_tm.ap()[:, j * D:j * D + D],
                                 in_=AP(self.xnt_d, j * D, [[NT * D, 128], [1, D]]))
                S("dxn").inc(d, 16)
                self.rec(("xn_in", j), S("dxn"))

    def pL2_out(self):
        S = self.S

        @self.G
        def _(g):
            g.wait_ge(S("dwsp").h, 16 * NI)
            g.wait_ge(S("v_pmax").h, self.mark["pmax2"])
            d = g.dma_start(out=self.gm2_d.ap(), in_=self.gmx.ap()[:, 1:2])
            S("dtap").inc(d, 16)
            g.wait_ge(S("dtap").h, S("dtap").n)

        @self.SY
        def _(sy):
            sy.wait_ge(S("dhsp").h, 16 * NI)

    def pL3_in(self):
        S = self.S

        @self.G
        def _(g):
            S("dconst").inc(g.dma_start(out=self.cxs.ap()[:, 0:1], in_=self.cx_d.ap()), 16)
            S("dconst").inc(g.dma_start(out=self.cxs.ap()[:, 1:2], in_=self.cd_d.ap()), 16)

        @self.SC
        def _(sc):
            d = sc.dma_start(out=self.Mrow.ap(), in_=self.mrow_d.ap())
            S("dmr").inc(d, 16)
            self.rec("mrow_in", S("dmr"))
            sc.wait_ge(S("dconst").h, 32)
            ins = sc.activation(self.sgc.ap()[:, 0:2], self.cxs.ap()[:, 0:2],
                                AF.Sigmoid)
            S("s_sig").inc(ins)
            for c in range(NI):
                if c >= 2:
                    sc.wait_ge(S("dhld").h, 16 * min(c, NI))
                d = sc.dma_start(
                    out=self.wdi.ap()[:, c * 1024:c * 1024 + 1024],
                    in_=AP(self.wdi_d, c * 1024, [[NI * 1024, 128], [1, 1024]]))
                S("dwf").inc(d, 16)
                self.rec(("wdiL", c), S("dwf"))
            self.rec("wdi_in", S("dwf"))

    def p0_L2(self):
        S = self.S

        @self.G
        def _(g):
            S("dconst").inc(g.dma_start(out=self.cxs.ap()[:, 0:1], in_=self.cx_d.ap()), 16)
            S("dconst").inc(g.dma_start(out=self.cxs.ap()[:, 1:2], in_=self.cd_d.ap()), 16)
            S("dconst").inc(g.dma_start(out=self.ident.ap(), in_=self.id_d.ap()), 16)
            S("dconst").inc(g.dma_start(out=self.identb.ap(), in_=self.id_d.ap()), 16)
        self.rec("const_done", S("dconst"))

        @self.SC
        def _(sc):
            sc.wait_ge(S("dconst").h, 32)
            ins = sc.activation(self.sgc.ap()[:, 0:2], self.cxs.ap()[:, 0:2], AF.Sigmoid)
            S("s_sig").inc(ins)

    def emit_tap(self, name, src_ap, waits):
        if name not in self.tap_d:
            return
        S = self.S

        @self.G
        def _(g):
            for semname, val in waits:
                g.wait_ge(S(semname).h, val)
            d = g.dma_start(out=self.tap_d[name].ap(), in_=src_ap)
            S("dtap").inc(d, 16)

    def build(self):
        T, NT, S = self.T, self.NT, self.S
        m = self.mode
        fs = "right" if m == "full" else None
        self.declare_dram()
        self.alloc_persistent()
        self.block = self.es.enter_context(self.nc.Block())
        scHX = ExitStack()
        if m in ("full", "L1", "L2"):
            self.hxq = scHX.enter_context(self.nc.sbuf_tensor("hxq", [128, ND * T], BF16))
        if m in ("full", "L1"):
            self.swd = self.es.enter_context(
                self.nc.sbuf_tensor("swd", [128, ND + 1], F32, side="right"))
            with ExitStack() as scQK:
                self.qf = scQK.enter_context(self.nc.sbuf_tensor("qf", [128, ND * T], BF16))
                self.kf = scQK.enter_context(self.nc.sbuf_tensor("kf", [128, ND * T], BF16))
                with ExitStack() as scA:
                    self.REGQ = self.ps("REGQ", [128, 1024], F32, scA)
                    self.REGK = self.ps("REGK", [128, 1024], F32, scA)
                    self.REGR = self.ps("REGR", [128, 1024], F32, scA)
                    self.REGT = self.ps("REGT", [128, 1024], F32, scA)
                    self.wqsb = self.sb("wqsb", [128, ND * D], BF16, scA)
                    self.wksb = self.sb("wksb", [128, ND * D], BF16, scA)
                    self.xr1 = self.sb("xr1", [128, 2 * D], F32, scA)
                    self.h1r = self.sb("h1r", [128, D], F32, scA)
                    self.qraw = self.sb("qraw", [128, 1024], BF16, scA)
                    self.kraw = self.sb("kraw", [128, 1024], BF16, scA)
                    self.rtmp = self.sb("rtmp", [128, 1024], BF16, scA)
                    self.trig = self.sb("trig", [128, 4 * T], BF16, scA)
                    self.rbuf = self.sb("rbuf", [128, 2048], F32, scA)
                    self.idxh = self.sb("idxh", [1, D], F32, scA)
                    self.posb = self.sb("posb", [128, T], F32, scA)
                    self.invfc = self.sb("invfc", [128, ND], F32, scA)
                    self.onec = self.sb("onec", [1, 1], F32, scA)
                    self.kif = self.sb("kif", [128, 512], I32, scA)
                    self.kff = self.sb("kff", [128, 512], F32, scA)
                    self.wdn = self.sb("wdn", [128, 3 * (IFF // 4)], F32, scA)
                    self.p0()
                    self.p12()
                    self.p3()
                with ExitStack() as scXN1:
                    if m == "full":
                        self.xn_tm = scXN1.enter_context(
                            self.nc.sbuf_tensor("xn_tm", [128, NT * D], BF16))
                    else:
                        self.xn_tm = self.es.enter_context(
                            self.nc.sbuf_tensor("xn_tm", [128, NT * D], BF16,
                                                side="right"))
                    with ExitStack() as scB:
                        self.SA = self.ps("SA", [128, 2048], F32, scB)
                        self.SB = self.ps("SB", [128, 2048], F32, scB)
                        self.wvsb = self.sb("wvsb", [128, ND * D], BF16, scB)
                        self.vsb = self.sb("vsb", [128, NT * D], BF16, scB)
                        self.wsb = self.sb("wsb", [128, T], F32, scB)
                        self.xr5 = self.sb("xr5", [128, 2 * D], F32, scB)
                        self.x2r = self.sb("x2r", [128, 2 * D], BF16, scB)
                        self.wtT = self.sb("wtT", [128, 2 * 128], BF16, scB)
                        self.p4()
                        self.p5()
                        self.p6_local()
                        if m == "full":
                            self.p6_cc()
                    if m == "full":
                        self.p6_scales()
                        with ExitStack() as scC:
                            self.PT0 = self.ps("PT0", [128, 512], BF16, scC)
                            self.PT1 = self.ps("PT1", [128, 512], BF16, scC)
                            self.i327 = self.sb("i327", [128, T], I32, scC)
                            self.p7()
            if m == "L1":
                self.pL1_out()
                self.emit_tap("h1", AP(self.hxq, 0, [[ND * T, 128], [1, ND * T]]),
                              [("v_h1e", self.mark.get("h1_done", 0))])
                self.emit_tap("qf", self.qf.ap(), [("v_qf", self.mark.get("qk_done_q", 0))])
                self.emit_tap("kf", self.kf.ap(), [("v_kf", self.mark.get("qk_done_k", 0))])
                self.emit_tap("v", self.vsb.ap(), [("s_ev", self.mark.get("v_all", 0))])
                scHX.close()
        if m == "L2":
            scXN2 = ExitStack()
            self.xn_tm = scXN2.enter_context(
                self.nc.sbuf_tensor("xn_tm", [128, self.NT * D], BF16, side="right"))
            self.p0_L2()
            self.p6_gather_in()
            self.p6_scales()
            with ExitStack() as scD:
                self.wdi = self.sb("wdi", [128, 2 * 1024], BF16, scD)
                self.Mrow = scD.enter_context(
                    self.nc.sbuf_tensor("Mrow", [1, D], F32))
                self.rwdb = self.sb("rwdb", [128, D], F32, scD)
                with ExitStack() as scE:
                    self.wgf = self.sb("wgf", [128, 3 * 1024], F32, scE)
                    self.wuf = self.sb("wuf", [128, 3 * 1024], F32, scE)
                    self.wdf = self.sb("wdf", [128, 3 * 1024], F32, scE)
                    self.wgn = self.sb("wgn", [128, 2 * 1024], F32, scE)
                    self.wun = self.sb("wun", [128, 2 * 1024], F32, scE)
                    self.rcg = self.sb("rcg", [128, 4], F32, scE)
                    self.rcu = self.sb("rcu", [128, 4], F32, scE)
                    self.swg = self.sb("swg", [128, 4], F32, scE)
                    self.swu = self.sb("swu", [128, 4], F32, scE)
                    self.scg = self.sb("scg", [128, 8], F32, scE)
                    self.scu = self.sb("scu", [128, 8], F32, scE)
                    self.rgrow = scE.enter_context(
                        self.nc.sbuf_tensor("rgrow", [1, 4 * 128], F32))
                    self.rurow = scE.enter_context(
                        self.nc.sbuf_tensor("rurow", [1, 4 * 128], F32))
                    self.rbg = self.sb("rbg", [128, 4 * 128], F32, scE)
                    self.rbu = self.sb("rbu", [128, 4 * 128], F32, scE)
                    self.i328 = self.sb("i328", [128, 1024], I32, scE)
                    self.wgi = self.sb("wgi", [128, 5 * 1024], BF16, scE)
                    self.wui = self.sb("wui", [128, 5 * 1024], BF16, scE)
                    self.silu_sb = self.sb("silu_sb", [128, 2 * 1024], F32, scE)
                    self.hst = self.sb("hst", [128, 2 * T], F32, scE)
                    self.pL2_in()
                    self.p8_head()
                    self.i327 = self.sb("i327", [128, 1024], I32, scE,
                                        side="right")
                    with ExitStack() as scC:
                        self.PT0 = self.ps("PT0", [128, 512], BF16, scC)
                        self.PT1 = self.ps("PT1", [128, 512], BF16, scC)
                        self.p7()
                    self.GA = self.ps("GA", [128, 1024], F32, scE)
                    self.GB = self.ps("GB", [128, 1024], F32, scE)
                    self.UA = self.ps("UA", [128, 1024], F32, scE)
                    self.UB = self.ps("UB", [128, 1024], F32, scE)
                    self.p8()
                    self.p9_local()
                self.pL2_out()
            scXN2.close()
        if m == "full":
            with ExitStack() as scD:
                self.wdi = self.sb("wdi", [128, NI * 1024], BF16, scD, side=fs)
                self.Mrow = scD.enter_context(
                    self.nc.sbuf_tensor("Mrow", [1, D], F32, side=fs))
                self.swdhrow = scD.enter_context(
                    self.nc.sbuf_tensor("swdhrow", [1, D], F32, side=fs))
                self.swdhb = self.sb("swdhb", [128, D], F32, scD, side=fs)
                with ExitStack() as scE:
                    self.GA = self.ps("GA", [128, 1024], F32, scE)
                    self.GB = self.ps("GB", [128, 1024], F32, scE)
                    self.UA = self.ps("UA", [128, 1024], F32, scE)
                    self.UB = self.ps("UB", [128, 1024], F32, scE)
                    self.wgf = self.sb("wgf", [128, 3 * 1024], F32, scE, side=fs)
                    self.wuf = self.sb("wuf", [128, 3 * 1024], F32, scE, side=fs)
                    self.wdf = self.sb("wdf", [128, 3 * 1024], F32, scE, side=fs)
                    self.wgn = self.sb("wgn", [128, 2 * 1024], F32, scE, side=fs)
                    self.wun = self.sb("wun", [128, 2 * 1024], F32, scE, side=fs)
                    self.rwdrow = scE.enter_context(
                        self.nc.sbuf_tensor("rwdrow", [1, D], F32, side=fs))
                    self.rwdb = self.sb("rwdb", [128, D], F32, scE, side=fs)
                    self.rcg = self.sb("rcg", [128, 4], F32, scE, side=fs)
                    self.rcu = self.sb("rcu", [128, 4], F32, scE, side=fs)
                    self.swg = self.sb("swg", [128, 4], F32, scE, side=fs)
                    self.swu = self.sb("swu", [128, 4], F32, scE, side=fs)
                    self.scg = self.sb("scg", [128, 8], F32, scE, side=fs)
                    self.scu = self.sb("scu", [128, 8], F32, scE, side=fs)
                    self.rgrow = scE.enter_context(
                        self.nc.sbuf_tensor("rgrow", [1, 4 * 128], F32, side=fs))
                    self.rurow = scE.enter_context(
                        self.nc.sbuf_tensor("rurow", [1, 4 * 128], F32, side=fs))
                    self.rbg = self.sb("rbg", [128, 4 * 128], F32, scE, side=fs)
                    self.rbu = self.sb("rbu", [128, 4 * 128], F32, scE, side=fs)
                    self.i328 = self.sb("i328", [128, 1024], I32, scE, side=fs)
                    self.wgi = self.sb("wgi", [128, 5 * 1024], BF16, scE, side=fs)
                    self.wui = self.sb("wui", [128, 5 * 1024], BF16, scE, side=fs)
                    self.silu_sb = self.sb("silu_sb", [128, 2 * 1024], F32, scE, side=fs)
                    self.hst = self.sb("hst", [128, 2 * T], F32, scE, side=fs)
                    self.p8()
                    self.p9_local()
                    self.p9_cc()
                self.p9_scales()
                scHX.close()
                with ExitStack() as scF:
                    self.alloc_scF(scF)
                    self.p10()
                    self.p11()
        if m == "L3":
            with ExitStack() as scD:
                self.wdi = self.sb("wdi", [128, NI * 1024], BF16, scD)
                self.Mrow = scD.enter_context(self.nc.sbuf_tensor("Mrow", [1, D], F32))
                self.swdhrow = scD.enter_context(
                    self.nc.sbuf_tensor("swdhrow", [1, D], F32))
                self.swdhb = self.sb("swdhb", [128, D], F32, scD)
                self.pL3_in()
                self.p9_gather_in()
                self.p9_scales()
                with ExitStack() as scF:
                    self.alloc_scF(scF)
                    self.p10()
                    self.p11()
        scHX.close()

    def alloc_scF(self, scF):
        T = self.T
        fs = "right" if self.mode == "full" else None
        self.hq = scF.enter_context(
            self.nc.sbuf_tensor("hq", [128, NI * T], BF16, side=fs))
        self.hld = self.sb("hld", [128, 2 * T], F32, scF, side=fs)
        self.i32F = self.sb("i32F", [128, 1024], I32, scF, side=fs)
        self.DPA = self.ps("DPA", [128, 1024], F32, scF)
        self.DPB = self.ps("DPB", [128, 1024], F32, scF)
        self.dnt = self.sb("dnt", [128, D], F32, scF, side=fs)
        self.x2l = self.sb("x2l", [128, 2 * D], BF16, scF, side=fs)
        self.outr = self.sb("outr", [128, 2 * D], F32, scF, side=fs)

def build_nc(T=2048, stage=99, taps=(), n_cores=8, mode="full"):
    nc = bass.Bass("TRN2", target_bir_lowering=False, debug=False)
    nc.detect_race_conditions = False
    es = ExitStack()
    with es:
        b = Builder(nc, es, T, stage, taps, n_cores=n_cores, mode=mode)
        b.build()
    return nc, b.tap_d


_NC_CACHE = {}


def _get_nc(T, mode):
    key = (T, mode)
    if key not in _NC_CACHE:
        _NC_CACHE[key] = build_nc(T=T, mode=mode)[0]
    return _NC_CACHE[key]


def kernel(L0_attn_x, L0_attn_wq, L0_attn_wk, L0_attn_wv, L0_ffn_clip_x,
           L0_ffn_clip_down, L0_ffn_w_gate, L0_ffn_w_up, L0_ffn_w_down):
    x = np.ascontiguousarray(np.asarray(L0_attn_x, dtype=np.float32))
    B, T, Dm = x.shape
    assert Dm == D and B == 8
    f32c = lambda a: np.ascontiguousarray(np.asarray(a, dtype=np.float32))
    wq, wk, wv = f32c(L0_attn_wq), f32c(L0_attn_wk), f32c(L0_attn_wv)
    wg, wu, wd = f32c(L0_ffn_w_gate), f32c(L0_ffn_w_up), f32c(L0_ffn_w_down)
    cxv = f32c(L0_ffn_clip_x).reshape(1)
    cdv = f32c(L0_ffn_clip_down).reshape(1)
    ident = np.eye(128, dtype=np.float32)
    P = np.zeros((128, 128), dtype=np.float32)
    ii = np.arange(0, 128, 2)
    P[ii, ii + 1] = -1.0
    P[ii + 1, ii] = 1.0
    protT = np.ascontiguousarray(P.T)
    mask128 = np.where(np.arange(128)[None, :] > np.arange(128)[:, None],
                       np.float32(-1e30), np.float32(0.0)).astype(np.float32)

    clips = {"clip_x": cxv, "clip_down": cdv}
    nc1 = _get_nc(T, "L1")
    shared1 = dict(clips, ident=ident, protT=protT, mask128=mask128,
                   wqT=np.ascontiguousarray(wq.T),
                   wkT=np.ascontiguousarray(wk.T), wvT=np.ascontiguousarray(wv.T),
                   wdN=wd)
    in1 = [dict(shared1, x=np.ascontiguousarray(x[b])) for b in range(B)]
    r1 = run_bass_kernel_spmd(nc1, in1, core_ids=list(range(B))).results
    gm8a = np.stack([np.asarray(r["gmax1_out"]).reshape(()) for r in r1])         .reshape(1, 8).astype(np.float32)

    nc2 = _get_nc(T, "L2")
    tile_w = lambda w: np.ascontiguousarray(
        w.T.reshape(8, 128, IFF // 128, 128).transpose(2, 1, 0, 3)).reshape(D, IFF)
    shared2 = dict(clips, ident=ident, wgT=tile_w(wg),
                   wuT=tile_w(wu), wdT=np.ascontiguousarray(wd.T),
                   wgN=wg, wuN=wu, gmax8=gm8a)
    in2 = [dict(shared2, xn_in=np.ascontiguousarray(np.asarray(r1[b]["xn_out"])),
                mrow_in=np.ascontiguousarray(np.asarray(r1[b]["mrow_out"])))
           for b in range(B)]
    r2 = run_bass_kernel_spmd(nc2, in2, core_ids=list(range(B))).results
    gm8b = np.stack([np.asarray(r["hmax_out"]).reshape(()) for r in r2])         .reshape(1, 8).astype(np.float32)

    nc3 = _get_nc(T, "L3")
    in3 = [dict(clips, hmax8=gm8b,
                h_in=np.ascontiguousarray(np.asarray(r2[b]["h_out"])),
                x2_in=np.ascontiguousarray(np.asarray(r1[b]["x2_out"])),
                wdi_in=np.ascontiguousarray(np.asarray(r2[b]["wdi_out"])),
                mrow_in=np.ascontiguousarray(np.asarray(r1[b]["mrow_out"])))
           for b in range(B)]
    r3 = run_bass_kernel_spmd(nc3, in3, core_ids=list(range(B))).results
    out = np.stack([np.asarray(r["out"]) for r in r3], axis=0)
    return out.astype(np.float32)
